# version 01
# baseline (speedup 1.0000x reference)
import numpy as np
from contextlib import ExitStack

import concourse.bass as bass
import concourse.mybir as mybir
from concourse.bass_utils import run_bass_kernel_spmd

F32 = mybir.dt.float32
BF16 = mybir.dt.bfloat16
AF = mybir.ActivationFunctionType
ALU = mybir.AluOpType
AX = mybir.AxisListType

NCORES = 8
D = 2048
T = 8192
TOK = T // NCORES
NT = 512
NG = TOK // NT
DKT = D // 128
FF0 = 5632
NFF0 = FF0 // 128
FF1 = 8192
NFF1 = FF1 // 128
NE = 8
EPS = 1e-5

ENGINES = ("tensor", "vector", "scalar", "gpsimd", "sync")


class Prog:

    def __init__(self):
        self.phases = []
        self.new_phase()

    def new_phase(self):
        self.cur = {e: [] for e in ENGINES}
        self.phases.append(self.cur)

    def op(self, eng, fn):
        self.cur[eng].append(("c", fn))

    def dma(self, eng, fn):
        self.cur[eng].append(("d", fn))


def emit(prog, sems):
    totals = []
    cum = {k: 0 for k in sems}
    for ph in prog.phases:
        for e in ENGINES:
            for kind, _fn in ph[e]:
                if kind == "c":
                    cum["c:" + e] += 1
                else:
                    cum["d:" + e] += 16
        totals.append(dict(cum))
    final = totals[-1]

    def body_for(e):
        def body(eng):
            last = {k: 0 for k in sems}
            for pi, ph in enumerate(prog.phases):
                ops = ph[e]
                if not ops:
                    continue
                if pi > 0:
                    tgt = totals[pi - 1]
                    for k, v in tgt.items():
                        if v > last[k]:
                            eng.wait_ge(sems[k], v)
                            last[k] = v
                for kind, fn in ops:
                    inst = fn(eng)
                    inst.then_inc(sems[("c:" if kind == "c" else "d:") + e],
                                  1 if kind == "c" else 16)
            for k, v in final.items():
                if v > last[k]:
                    eng.wait_ge(sems[k], v)
                    last[k] = v
        return body
    return body_for


def _finish(nc, st, prog):
    sems = {}
    for e in ENGINES:
        sems["c:" + e] = st.enter_context(nc.semaphore("sc_" + e))
        sems["d:" + e] = st.enter_context(nc.semaphore("sd_" + e))
    blk = st.enter_context(nc.Block())
    body_for = emit(prog, sems)
    blk.tensor(body_for("tensor"))
    blk.vector(body_for("vector"))
    blk.scalar(body_for("scalar"))
    blk.gpsimd(body_for("gpsimd"))
    blk.sync(body_for("sync"))


def ln_group(prog, c, y_ap, out_write, nkt=DKT):
    ones = c["ones"]
    s1, s2, pA, pB = c["pb_s1"], c["pb_s2"], c["pb_A"], c["pb_B"]
    stats = c["stats"]
    sq = c["sq"]
    inv_d = 1.0 / (128.0 * nkt)

    for kt in range(nkt):
        prog.op("tensor", lambda e, kt=kt: e.matmul(
            s1[0:1, :], ones[:, 0:1], y_ap(kt), start=(kt == 0), stop=(kt == nkt - 1)))
    prog.new_phase()
    for kt in range(nkt + 1):
        if kt < nkt:
            prog.op("scalar", lambda e, kt=kt: e.activation(
                sq[kt % 2][:, :], y_ap(kt), AF.Square))
        if kt > 0:
            prog.op("tensor", lambda e, kt=kt: e.matmul(
                s2[0:1, :], ones[:, 0:1], sq[(kt - 1) % 2][:, :],
                start=(kt - 1 == 0), stop=(kt - 1 == nkt - 1)))
        prog.new_phase()
    mu = stats[:, 0 * NT:1 * NT]
    ex2 = stats[:, 1 * NT:2 * NT]
    tmp = stats[:, 2 * NT:3 * NT]
    rstd = stats[:, 3 * NT:4 * NT]
    prog.op("scalar", lambda e: e.activation(mu, s1[0:1, :], AF.Identity, scale=inv_d))
    prog.op("vector", lambda e: e.tensor_scalar_mul(ex2, s2[0:1, :], inv_d))
    prog.new_phase()
    prog.op("scalar", lambda e: e.activation(tmp, mu, AF.Square))
    prog.new_phase()
    prog.op("vector", lambda e: e.tensor_sub(ex2, ex2, tmp))
    prog.new_phase()
    prog.op("scalar", lambda e: e.activation(tmp, ex2, AF.Sqrt, bias=c["eps"]))
    prog.new_phase()
    prog.op("vector", lambda e: e.reciprocal(rstd, tmp))
    prog.new_phase()
    prog.op("vector", lambda e: e.tensor_mul(mu, mu, rstd))
    prog.new_phase()
    prog.op("tensor", lambda e: e.matmul(pA[:, :], ones[0:1, :], rstd, start=True, stop=True))
    prog.op("tensor", lambda e: e.matmul(pB[:, :], ones[0:1, :], mu, start=True, stop=True))
    prog.new_phase()
    pending = None
    for kt in range(nkt + 2):
        if pending is not None:
            pending(prog)
            pending = None
        if kt < nkt:
            prog.op("vector", lambda e, kt=kt: e.tensor_mul(sq[kt % 2][:, :], y_ap(kt), pA[:, :]))
        if 0 < kt <= nkt:
            dst, post = out_write(kt - 1)
            prog.op("vector", lambda e, kt=kt, dst=dst: e.tensor_sub(
                dst, sq[(kt - 1) % 2][:, :], pB[:, :]))
            pending = post
        prog.new_phase()


def build_A():
    nc = bass.Bass("TRN2", target_bir_lowering=False)
    x_ext = nc.dram_tensor("x_ext", [D, TOK + 1], F32, kind="ExternalInput").ap()
    wactT = nc.dram_tensor("wactT", [2 * D, D], F32, kind="ExternalInput").ap()
    wrecT = nc.dram_tensor("wrecT", [2 * D, D], F32, kind="ExternalInput").ap()
    xvT = nc.dram_tensor("xvT", [D, TOK], F32, kind="ExternalOutput").ap()
    r4T = nc.dram_tensor("r4T", [D, TOK], F32, kind="ExternalOutput").ap()

    xr = x_ext.rearrange("(kt p) t -> p kt t", p=128)
    war = wactT.rearrange("(kt p) c -> p kt c", p=128)
    wrr = wrecT.rearrange("(kt p) c -> p kt c", p=128)
    xvr = xvT.rearrange("(ct p) t -> p ct t", p=128)
    r4r = r4T.rearrange("(ct p) t -> p ct t", p=128)

    prog = Prog()
    with ExitStack() as st:
        sb = lambda n, s, d: st.enter_context(nc.sbuf_tensor(n, s, d))
        stage = sb("stage", [128, TOK + 1], F32)
        xbf = sb("xbf", [128, DKT, TOK + 1], BF16)
        wsl = sb("wsl", [128, 32, 128], F32)
        wbf = sb("wbf", [128, 32, 128], BF16)
        outsl = sb("outsl", [128, DKT, TOK], F32)
        pb = [st.enter_context(nc.psum_tensor("pb%d" % i, [128, 512], F32)) for i in range(2)]

        for kt in range(DKT):
            prog.dma("sync", lambda e, kt=kt: e.dma_start(out=stage[:, :], in_=xr[:, kt, :]))
            prog.new_phase()
            prog.op("scalar", lambda e, kt=kt: e.copy(out=xbf[:, kt, :], in_=stage[:, :]))
            prog.new_phase()

        def rhs_for(k, g):
            if k < DKT:
                return xbf[:, k, g * 512:(g + 1) * 512]
            return xbf[:, k - DKT, 1 + g * 512:1 + (g + 1) * 512]

        def wpass(wr_ap, out_r, square_twice):
            for ct in range(DKT):
                for hh in range(2):
                    prog.dma("sync", lambda e, ct=ct, hh=hh: e.dma_start(
                        out=wsl[:, hh * 16:(hh + 1) * 16, :],
                        in_=wr_ap[:, hh * 16:(hh + 1) * 16, ct * 128:(ct + 1) * 128]))
                prog.new_phase()
                prog.op("scalar", lambda e: e.copy(
                    out=wbf[:, :, :].rearrange("p a b -> p (a b)"),
                    in_=wsl[:, :, :].rearrange("p a b -> p (a b)")))
                prog.new_phase()
                for g in range(2):
                    for k in range(32):
                        prog.op("tensor", lambda e, g=g, k=k: e.matmul(
                            pb[g][:, :], wbf[:, k, :], rhs_for(k, g),
                            start=(k == 0), stop=(k == 31)))
                prog.new_phase()
                for g in range(2):
                    prog.op("scalar", lambda e, ct=ct, g=g: e.activation(
                        outsl[:, ct, g * 512:(g + 1) * 512], pb[g][:, :], AF.Sigmoid))
                prog.new_phase()
                for g in range(2):
                    prog.op("vector", lambda e, ct=ct, g=g: e.tensor_scalar_max(
                        outsl[:, ct, g * 512:(g + 1) * 512],
                        outsl[:, ct, g * 512:(g + 1) * 512], 0.5))
                prog.new_phase()
                if square_twice:
                    for _ in range(2):
                        prog.op("scalar", lambda e, ct=ct: e.activation(
                            outsl[:, ct, :], outsl[:, ct, :], AF.Square))
                        prog.new_phase()
            for ch in range(4):
                prog.dma("gpsimd", lambda e, ch=ch: e.dma_start(
                    out=out_r[:, ch * 4:(ch + 1) * 4, :],
                    in_=outsl[:, ch * 4:(ch + 1) * 4, :]))
            prog.new_phase()

        wpass(war, xvr, False)
        wpass(wrr, r4r, True)
        _finish(nc, st, prog)
    return nc


def build_B():
    nc = bass.Bass("TRN2", target_bir_lowering=False)
    re_in = nc.dram_tensor("re_in", [128, T], F32, kind="ExternalInput").ap()
    im_in = nc.dram_tensor("im_in", [128, T], F32, kind="ExternalInput").ap()
    re_out = nc.dram_tensor("re_out", [128, T], F32, kind="ExternalOutput").ap()
    im_out = nc.dram_tensor("im_out", [128, T], F32, kind="ExternalOutput").ap()

    prog = Prog()
    with ExitStack() as st:
        sb = lambda n: st.enter_context(nc.sbuf_tensor(n, [128, T], F32))
        rA, iA, rB, iB, tmp = sb("rA"), sb("iA"), sb("rB"), sb("iB"), sb("tmp")

        prog.dma("sync", lambda e: e.dma_start(out=rA[:, :], in_=re_in[:, :]))
        prog.dma("sync", lambda e: e.dma_start(out=iA[:, :], in_=im_in[:, :]))
        prog.new_phase()
        prog.op("vector", lambda e: e.tensor_mul(tmp[:, :], rA[:, :], rA[:, :]))
        prog.op("gpsimd", lambda e: e.tensor_mul(rB[:, :], iA[:, :], iA[:, :]))
        prog.new_phase()
        prog.op("vector", lambda e: e.tensor_add(tmp[:, :], tmp[:, :], rB[:, :]))
        prog.new_phase()
        prog.op("scalar", lambda e: e.activation(tmp[:, :], tmp[:, :], AF.Sqrt))
        prog.new_phase()
        prog.op("vector", lambda e: e.tensor_tensor_scan(
            rB[:, :], tmp[:, :], tmp[:, :], 0.0, ALU.max, ALU.bypass))
        prog.new_phase()
        prog.op("vector", lambda e: e.reciprocal(tmp[:, :], rB[:, :]))
        prog.new_phase()
        prog.op("vector", lambda e: e.tensor_mul(rA[:, :], rA[:, :], tmp[:, :]))
        prog.op("gpsimd", lambda e: e.tensor_mul(iA[:, :], iA[:, :], tmp[:, :]))
        prog.new_phase()
        bufs = [rA, iA, rB, iB]
        k = 1
        while k < T:
            ra, ia, rb, ib = bufs
            prog.op("vector", lambda e, k=k, ra=ra, rb=rb: e.tensor_mul(
                rb[:, k:], ra[:, k:], ra[:, :T - k]))
            prog.op("gpsimd", lambda e, k=k, ra=ra, ia=ia, ib=ib: e.tensor_mul(
                ib[:, k:], ra[:, k:], ia[:, :T - k]))
            prog.op("scalar", lambda e, k=k, ra=ra, rb=rb: e.copy(
                out=rb[:, :k], in_=ra[:, :k]))
            prog.new_phase()
            prog.op("vector", lambda e, k=k, ia=ia: e.tensor_mul(
                tmp[:, k:], ia[:, k:], ia[:, :T - k]))
            prog.op("scalar", lambda e, k=k, ia=ia, ib=ib: e.copy(
                out=ib[:, :k], in_=ia[:, :k]))
            prog.new_phase()
            prog.op("vector", lambda e, k=k, rb=rb: e.tensor_sub(
                rb[:, k:], rb[:, k:], tmp[:, k:]))
            prog.new_phase()
            prog.op("vector", lambda e, k=k, ia=ia, ra=ra: e.tensor_mul(
                tmp[:, k:], ia[:, k:], ra[:, :T - k]))
            prog.new_phase()
            prog.op("vector", lambda e, k=k, ib=ib: e.tensor_add(
                ib[:, k:], ib[:, k:], tmp[:, k:]))
            prog.new_phase()
            bufs = [rb, ib, ra, ia]
            k *= 2
        ra, ia = bufs[0], bufs[1]
        prog.dma("gpsimd", lambda e, ra=ra: e.dma_start(out=re_out[:, :], in_=ra[:, :]))
        prog.dma("gpsimd", lambda e, ia=ia: e.dma_start(out=im_out[:, :], in_=ia[:, :]))
        prog.new_phase()
        _finish(nc, st, prog)
    return nc


def build_C():
    nc = bass.Bass("TRN2", target_bir_lowering=False)
    xT = nc.dram_tensor("xT", [D, TOK], F32, kind="ExternalInput").ap()
    kvT = nc.dram_tensor("kvT", [D, TOK], F32, kind="ExternalInput").ap()
    r4T = nc.dram_tensor("r4T", [D, TOK], F32, kind="ExternalInput").ap()
    wvalT = nc.dram_tensor("wvalT", [D, D], F32, kind="ExternalInput").ap()
    ones_d = nc.dram_tensor("ones", [128, 128], F32, kind="ExternalInput").ap()
    x1T = nc.dram_tensor("x1T", [D, TOK], F32, kind="ExternalOutput").ap()

    xrr = xT.rearrange("(kt p) t -> p kt t", p=128)
    kvr = kvT.rearrange("(kt p) t -> p kt t", p=128)
    r4r = r4T.rearrange("(kt p) t -> p kt t", p=128)
    wvr = wvalT.rearrange("(kt p) c -> p kt c", p=128)
    x1r = x1T.rearrange("(kt p) t -> p kt t", p=128)

    prog = Prog()
    with ExitStack() as st:
        sb = lambda n, s, d: st.enter_context(nc.sbuf_tensor(n, s, d))
        xsl = sb("xsl", [128, DKT, TOK], F32)
        r4sl = sb("r4sl", [128, DKT, TOK], F32)
        kvbf = sb("kvbf", [128, DKT, TOK], BF16)
        stage = sb("stage", [128, TOK], F32)
        wsl = sb("wsl", [128, 8, 128], F32)
        wbf = sb("wbf", [128, DKT, 128], BF16)
        scr = sb("scr", [128, 2, 512], F32)
        sq = [sb("sq0", [128, NT], F32), sb("sq1", [128, NT], F32)]
        stats = sb("stats", [1, 4 * NT], F32)
        ones = sb("onesb", [128, 128], F32)
        eps = sb("eps", [1, 1], F32)
        pb = [st.enter_context(nc.psum_tensor("pb%d" % i, [128, 512], F32)) for i in range(6)]

        prog.op("vector", lambda e: e.memset(eps[:, :], EPS))
        for ch in range(4):
            prog.dma("sync", lambda e, ch=ch: e.dma_start(
                out=xsl[:, ch * 4:(ch + 1) * 4, :], in_=xrr[:, ch * 4:(ch + 1) * 4, :]))
            prog.dma("sync", lambda e, ch=ch: e.dma_start(
                out=r4sl[:, ch * 4:(ch + 1) * 4, :], in_=r4r[:, ch * 4:(ch + 1) * 4, :]))
        prog.dma("sync", lambda e: e.dma_start(out=ones[:, :], in_=ones_d[:, :]))
        prog.new_phase()
        for kt in range(DKT):
            prog.dma("sync", lambda e, kt=kt: e.dma_start(out=stage[:, :], in_=kvr[:, kt, :]))
            prog.new_phase()
            prog.op("scalar", lambda e, kt=kt: e.copy(out=kvbf[:, kt, :], in_=stage[:, :]))
            prog.new_phase()
        for ct in range(DKT):
            for hh in range(2):
                prog.dma("sync", lambda e, ct=ct, hh=hh: e.dma_start(
                    out=wsl[:, :, :],
                    in_=wvr[:, hh * 8:(hh + 1) * 8, ct * 128:(ct + 1) * 128]))
                prog.new_phase()
                prog.op("scalar", lambda e, hh=hh: e.copy(
                    out=wbf[:, hh * 8:(hh + 1) * 8, :].rearrange("p a b -> p (a b)"),
                    in_=wsl[:, :, :].rearrange("p a b -> p (a b)")))
                prog.new_phase()
            for g in range(2):
                for k in range(DKT):
                    prog.op("tensor", lambda e, g=g, k=k: e.matmul(
                        pb[g][:, :], wbf[:, k, :], kvbf[:, k, g * 512:(g + 1) * 512],
                        start=(k == 0), stop=(k == DKT - 1)))
            prog.new_phase()
            for g in range(2):
                prog.op("vector", lambda e, ct=ct, g=g: e.tensor_mul(
                    scr[:, g, :], pb[g][:, :], r4sl[:, ct, g * 512:(g + 1) * 512]))
            prog.new_phase()
            for g in range(2):
                prog.op("gpsimd", lambda e, ct=ct, g=g: e.tensor_add(
                    xsl[:, ct, g * 512:(g + 1) * 512],
                    xsl[:, ct, g * 512:(g + 1) * 512], scr[:, g, :]))
            prog.new_phase()
        lnc = {"ones": ones, "pb_s1": pb[2], "pb_s2": pb[3], "pb_A": pb[4], "pb_B": pb[5],
               "stats": stats, "sq": sq, "eps": eps[0:1, 0:1]}
        for g in range(NG):
            def out_write(kt, g=g):
                return r4sl[:, kt, g * NT:(g + 1) * NT], None
            ln_group(prog, lnc, lambda kt, g=g: xsl[:, kt, g * NT:(g + 1) * NT], out_write)
        for ch in range(4):
            prog.dma("gpsimd", lambda e, ch=ch: e.dma_start(
                out=x1r[:, ch * 4:(ch + 1) * 4, :], in_=r4sl[:, ch * 4:(ch + 1) * 4, :]))
        prog.new_phase()
        _finish(nc, st, prog)
    return nc


def build_D1():
    nc = bass.Bass("TRN2", target_bir_lowering=False)
    x1T = nc.dram_tensor("x1T", [D, TOK], F32, kind="ExternalInput").ap()
    wgT = nc.dram_tensor("wgT", [D, FF0], F32, kind="ExternalInput").ap()
    wuT = nc.dram_tensor("wuT", [D, FF0], F32, kind="ExternalInput").ap()
    wdT = nc.dram_tensor("wdT", [FF0, D], F32, kind="ExternalInput").ap()
    gbT = nc.dram_tensor("gbT", [FF0, TOK], F32, kind="ExternalOutput").ap()
    ubT = nc.dram_tensor("ubT", [FF0, TOK], F32, kind="ExternalOutput").ap()
    ybT = nc.dram_tensor("ybT", [D, TOK], F32, kind="ExternalOutput").ap()

    x1r = x1T.rearrange("(kt p) t -> p kt t", p=128)
    wgr = wgT.rearrange("(kt p) f -> p kt f", p=128)
    wur = wuT.rearrange("(kt p) f -> p kt f", p=128)
    wdr = wdT.rearrange("(ft p) d -> p ft d", p=128)
    gbr = gbT.rearrange("(ft p) t -> p ft t", p=128)
    ubr = ubT.rearrange("(ft p) t -> p ft t", p=128)
    ybr = ybT.rearrange("(kt p) t -> p kt t", p=128)

    prog = Prog()
    with ExitStack() as st:
        sb = lambda n, s, d: st.enter_context(nc.sbuf_tensor(n, s, d))
        x1bf = sb("x1bf", [128, DKT, TOK], BF16)
        stage = sb("stage", [128, DKT, 128], F32)
        wgbf = sb("wgbf", [128, DKT, 128], BF16)
        wubf = sb("wubf", [128, DKT, 128], BF16)
        wdbf = sb("wdbf", [128, NFF0, 128], BF16)
        base_h = sb("base_h", [128, NFF0, NT], BF16)
        gb = sb("gb", [128, 2, NT], F32)
        ub = sb("ub", [128, 2, NT], F32)
        silu_t = sb("silu_t", [128, NT], F32)
        xres = sb("xres", [128, NT], F32)
        oscr = sb("oscr", [128, NT], F32)
        pb = [st.enter_context(nc.psum_tensor("pb%d" % i, [128, 512], F32)) for i in range(3)]

        for qq in range(4):
            for kk in range(4):
                prog.dma("sync", lambda e, qq=qq, kk=kk: e.dma_start(
                    out=stage[:, kk * 4:(kk + 1) * 4, :].rearrange("p a b -> p (a b)"),
                    in_=x1r[:, qq * 4 + kk, 0:NT]))
            prog.new_phase()
            prog.op("scalar", lambda e, qq=qq: e.copy(
                out=x1bf[:, qq * 4:(qq + 1) * 4, 0:NT],
                in_=stage[:, :, :].rearrange("p (c dd) b -> p c (dd b)", c=4)))
            prog.new_phase()
            for kk in range(4):
                prog.dma("sync", lambda e, qq=qq, kk=kk: e.dma_start(
                    out=stage[:, kk * 4:(kk + 1) * 4, :].rearrange("p a b -> p (a b)"),
                    in_=x1r[:, qq * 4 + kk, NT:TOK]))
            prog.new_phase()
            prog.op("scalar", lambda e, qq=qq: e.copy(
                out=x1bf[:, qq * 4:(qq + 1) * 4, NT:TOK],
                in_=stage[:, :, :].rearrange("p (c dd) b -> p c (dd b)", c=4)))
            prog.new_phase()

        for g in range(NG):
            gs = slice(g * NT, (g + 1) * NT)
            for ft in range(NFF0):
                fs = slice(ft * 128, (ft + 1) * 128)
                prog.dma("sync", lambda e, fs=fs: e.dma_start(out=stage[:, :, :],
                                                              in_=wgr[:, :, fs]))
                prog.new_phase()
                prog.op("scalar", lambda e: e.copy(
                    out=wgbf[:, :, :].rearrange("p a b -> p (a b)"),
                    in_=stage[:, :, :].rearrange("p a b -> p (a b)")))
                prog.new_phase()
                prog.dma("sync", lambda e, fs=fs: e.dma_start(out=stage[:, :, :],
                                                              in_=wur[:, :, fs]))
                for k in range(DKT):
                    prog.op("tensor", lambda e, k=k, gs=gs: e.matmul(
                        pb[0][:, :], wgbf[:, k, :], x1bf[:, k, gs],
                        start=(k == 0), stop=(k == DKT - 1)))
                prog.new_phase()
                prog.op("scalar", lambda e: e.copy(
                    out=wubf[:, :, :].rearrange("p a b -> p (a b)"),
                    in_=stage[:, :, :].rearrange("p a b -> p (a b)")))
                prog.op("vector", lambda e: e.tensor_copy(out=gb[:, 0, :], in_=pb[0][:, :]))
                prog.new_phase()
                for k in range(DKT):
                    prog.op("tensor", lambda e, k=k, gs=gs: e.matmul(
                        pb[1][:, :], wubf[:, k, :], x1bf[:, k, gs],
                        start=(k == 0), stop=(k == DKT - 1)))
                prog.op("scalar", lambda e: e.activation(silu_t[:, :], gb[:, 0, :], AF.Silu))
                prog.new_phase()
                prog.op("vector", lambda e: e.tensor_copy(out=ub[:, 0, :], in_=pb[1][:, :]))
                prog.new_phase()
                prog.op("vector", lambda e, ft=ft: e.tensor_mul(
                    base_h[:, ft, :], silu_t[:, :], ub[:, 0, :]))
                prog.dma("gpsimd", lambda e, ft=ft, gs=gs: e.dma_start(
                    out=gbr[:, ft, gs], in_=gb[:, 0, :]))
                prog.dma("gpsimd", lambda e, ft=ft, gs=gs: e.dma_start(
                    out=ubr[:, ft, gs], in_=ub[:, 0, :]))
                prog.new_phase()
            for dt in range(DKT):
                ds = slice(dt * 128, (dt + 1) * 128)
                for ch in range(6):
                    ftsl = slice(ch * 8, min((ch + 1) * 8, NFF0))
                    n = ftsl.stop - ftsl.start
                    prog.dma("sync", lambda e, ds=ds, ftsl=ftsl, n=n: e.dma_start(
                        out=stage[:, 0:n, :], in_=wdr[:, ftsl, ds]))
                    prog.new_phase()
                    prog.op("scalar", lambda e, ftsl=ftsl, n=n: e.copy(
                        out=wdbf[:, ftsl, :].rearrange("p a b -> p (a b)"),
                        in_=stage[:, 0:n, :].rearrange("p a b -> p (a b)")))
                    prog.new_phase()
                prog.dma("sync", lambda e, dt=dt, gs=gs: e.dma_start(
                    out=xres[:, :], in_=x1r[:, dt, gs]))
                prog.new_phase()
                for ft in range(NFF0):
                    prog.op("tensor", lambda e, ft=ft: e.matmul(
                        pb[2][:, :], wdbf[:, ft, :], base_h[:, ft, :],
                        start=(ft == 0), stop=(ft == NFF0 - 1)))
                prog.new_phase()
                prog.op("vector", lambda e: e.tensor_add(oscr[:, :], pb[2][:, :], xres[:, :]))
                prog.new_phase()
                prog.dma("gpsimd", lambda e, dt=dt, gs=gs: e.dma_start(
                    out=ybr[:, dt, gs], in_=oscr[:, :]))
                prog.new_phase()
        _finish(nc, st, prog)
    return nc


def build_D2():
    nc = bass.Bass("TRN2", target_bir_lowering=False)
    x1T = nc.dram_tensor("x1T", [D, TOK], F32, kind="ExternalInput").ap()
    gbT = nc.dram_tensor("gbT", [FF0, TOK], F32, kind="ExternalInput").ap()
    ubT = nc.dram_tensor("ubT", [FF0, TOK], F32, kind="ExternalInput").ap()
    ybT = nc.dram_tensor("ybT", [D, TOK], F32, kind="ExternalInput").ap()
    wrT = nc.dram_tensor("wrT", [D, 8], F32, kind="ExternalInput").ap()
    agT = nc.dram_tensor("agT", [NE * D, 16], F32, kind="ExternalInput").ap()
    auT = nc.dram_tensor("auT", [NE * D, 16], F32, kind="ExternalInput").ap()
    bgT = nc.dram_tensor("bgT", [NE * 16, FF0], F32, kind="ExternalInput").ap()
    buT = nc.dram_tensor("buT", [NE * 16, FF0], F32, kind="ExternalInput").ap()
    adT = nc.dram_tensor("adT", [NE * FF0, 16], F32, kind="ExternalInput").ap()
    bdT = nc.dram_tensor("bdT", [NE * 16, D], F32, kind="ExternalInput").ap()
    sel_d = nc.dram_tensor("sel", [8, 128], F32, kind="ExternalInput").ap()
    ident_d = nc.dram_tensor("ident", [128, 128], F32, kind="ExternalInput").ap()
    ones_d = nc.dram_tensor("ones", [128, 128], F32, kind="ExternalInput").ap()
    x2T = nc.dram_tensor("x2T", [D, TOK], F32, kind="ExternalOutput").ap()

    x1r = x1T.rearrange("(kt p) t -> p kt t", p=128)
    gbr = gbT.rearrange("(ft p) t -> p ft t", p=128)
    ubr = ubT.rearrange("(ft p) t -> p ft t", p=128)
    ybr = ybT.rearrange("(kt p) t -> p kt t", p=128)
    wrr = wrT.rearrange("(kt p) m -> p kt m", p=128)
    agr = agT.rearrange("(ekt p) m -> p ekt m", p=128)
    aur = auT.rearrange("(ekt p) m -> p ekt m", p=128)
    bgr = bgT.rearrange("(e p) f -> p e f", p=16)
    bur = buT.rearrange("(e p) f -> p e f", p=16)
    adr = adT.rearrange("(e ft p) m -> p e ft m", p=128, ft=NFF0)
    bdr = bdT.rearrange("(e p) d -> p e d", p=16)
    x2r = x2T.rearrange("(kt p) t -> p kt t", p=128)

    prog = Prog()
    with ExitStack() as st:
        sb = lambda n, s, d: st.enter_context(nc.sbuf_tensor(n, s, d))
        x1bf = sb("x1bf", [128, DKT, NT], BF16)
        ybuf = sb("ybuf", [128, DKT, NT], F32)
        stage = sb("stage", [128, 4, 128], F32)
        wrbf = sb("wrbf", [128, DKT, 8], BF16)
        agbf = sb("agbf", [128, DKT, 16], BF16)
        bst = sb("bst", [16, NE, 128], F32)
        bgbf = sb("bgbf", [16, NE, 128], BF16)
        bubf = sb("bubf", [16, NE, 128], BF16)
        adst = sb("adst", [128, NE, 16], F32)
        adbf = sb("adbf", [128, NE, 16], BF16)
        bdbf = sb("bdbf", [16, NE, 128], BF16)
        tg_bf = sb("tg_bf", [16, NE, NT], BF16)
        tu_bf = sb("tu_bf", [16, NE, NT], BF16)
        s_acc = sb("s_acc", [16, NE, NT], F32)
        s_bf = sb("s_bf", [16, NE, NT], BF16)
        we8 = sb("we8", [8, NT], F32)
        probs = sb("probs", [128, 32], F32)
        probs2 = sb("probs2", [128, 32], F32)
        misc = sb("misc", [128, 16], F32)
        gb = sb("gb", [128, NT], F32)
        ub = sb("ub", [128, NT], F32)
        hg = sb("hg", [128, NT], F32)
        hu = sb("hu", [128, NT], F32)
        hbf = sb("hbf", [128, NT], BF16)
        xres = sb("xres", [128, NT], F32)
        sq = [sb("sq0", [128, NT], F32), sb("sq1", [128, NT], F32)]
        stats = sb("stats", [1, 4 * NT], F32)
        ones = sb("onesb", [128, 128], F32)
        ident = sb("identb", [128, 128], F32)
        sel = sb("selb", [8, 128], F32)
        outt = sb("outt", [128, 2, NT], F32)
        eps = sb("eps", [1, 1], F32)
        pb = [st.enter_context(nc.psum_tensor("pb%d" % i, [128, 512], F32)) for i in range(8)]

        prog.op("vector", lambda e: e.memset(eps[:, :], EPS))
        prog.dma("sync", lambda e: e.dma_start(out=ones[:, :], in_=ones_d[:, :]))
        prog.dma("sync", lambda e: e.dma_start(out=ident[:, :], in_=ident_d[:, :]))
        prog.dma("sync", lambda e: e.dma_start(out=sel[:, :], in_=sel_d[:, :]))
        prog.dma("sync", lambda e: e.dma_start(
            out=stage[:, 0, :].rearrange("p (a m) -> p a m", m=8), in_=wrr[:, :, :]))
        prog.new_phase()
        prog.op("scalar", lambda e: e.copy(
            out=wrbf[:, :, :].rearrange("p a m -> p (a m)"),
            in_=stage[:, 0, :]))
        prog.new_phase()

        for g in range(NG):
            gs = slice(g * NT, (g + 1) * NT)
            for kt in range(DKT):
                prog.dma("sync", lambda e, kt=kt, gs=gs: e.dma_start(
                    out=sq[0][:, :], in_=x1r[:, kt, gs]))
                prog.new_phase()
                prog.op("scalar", lambda e, kt=kt: e.copy(out=x1bf[:, kt, :], in_=sq[0][:, :]))
                prog.new_phase()
            for sub in range(4):
                for k in range(DKT):
                    prog.op("tensor", lambda e, sub=sub, k=k: e.matmul(
                        pb[sub][:, 0:8], x1bf[:, k, sub * 128:(sub + 1) * 128],
                        wrbf[:, k, :], start=(k == 0), stop=(k == DKT - 1)))
            prog.new_phase()
            for sub in range(4):
                prog.op("vector", lambda e, sub=sub: e.reduce_max(
                    misc[:, sub:sub + 1], pb[sub][:, 0:8], AX.X))
            prog.new_phase()
            for sub in range(4):
                prog.op("vector", lambda e, sub=sub: e.tensor_scalar(
                    probs[:, sub * 8:(sub + 1) * 8], pb[sub][:, 0:8],
                    misc[:, sub:sub + 1], None, ALU.subtract))
            prog.new_phase()
            prog.op("scalar", lambda e: e.activation(probs[:, :], probs[:, :], AF.Exp))
            prog.new_phase()
            for sub in range(4):
                prog.op("vector", lambda e, sub=sub: e.reduce_sum(
                    misc[:, 4 + sub:5 + sub], probs[:, sub * 8:(sub + 1) * 8], AX.X))
            prog.new_phase()
            prog.op("vector", lambda e: e.reciprocal(misc[:, 4:8], misc[:, 4:8]))
            prog.new_phase()
            for sub in range(4):
                prog.op("vector", lambda e, sub=sub: e.tensor_scalar_mul(
                    probs[:, sub * 8:(sub + 1) * 8], probs[:, sub * 8:(sub + 1) * 8],
                    misc[:, 4 + sub:5 + sub]))
            prog.new_phase()
            for sub in range(4):
                prog.op("vector", lambda e, sub=sub: e.reduce_max(
                    misc[:, sub:sub + 1], probs[:, sub * 8:(sub + 1) * 8], AX.X))
            prog.new_phase()
            for sub in range(4):
                prog.op("vector", lambda e, sub=sub: e.tensor_scalar(
                    probs2[:, sub * 8:(sub + 1) * 8], probs[:, sub * 8:(sub + 1) * 8],
                    misc[:, sub:sub + 1], -1.0e9, ALU.is_ge, ALU.mult))
            prog.new_phase()
            prog.op("vector", lambda e: e.tensor_add(probs2[:, :], probs2[:, :], probs[:, :]))
            prog.new_phase()
            for sub in range(4):
                prog.op("vector", lambda e, sub=sub: e.reduce_max(
                    misc[:, 8 + sub:9 + sub], probs2[:, sub * 8:(sub + 1) * 8], AX.X))
            prog.new_phase()
            for sub in range(4):
                prog.op("vector", lambda e, sub=sub: e.tensor_scalar(
                    probs2[:, sub * 8:(sub + 1) * 8], probs[:, sub * 8:(sub + 1) * 8],
                    misc[:, 8 + sub:9 + sub], None, ALU.is_ge))
            prog.new_phase()
            prog.op("vector", lambda e: e.tensor_mul(probs[:, :], probs[:, :], probs2[:, :]))
            prog.new_phase()
            for sub in range(4):
                prog.op("tensor", lambda e, sub=sub: e.matmul(
                    pb[4][0:8, sub * 128:(sub + 1) * 128], probs[:, sub * 8:(sub + 1) * 8],
                    ident[:, :], start=(sub == 0), stop=(sub == 3), skip_group_check=True))
            prog.new_phase()
            prog.op("vector", lambda e: e.tensor_copy(out=we8[:, :], in_=pb[4][0:8, :]))
            prog.new_phase()
            for e_ in range(NE):
                for w_ in range(2):
                    src = agr if w_ == 0 else aur
                    dst = tg_bf if w_ == 0 else tu_bf
                    prog.dma("sync", lambda e, e_=e_, src=src: e.dma_start(
                        out=stage[:, 0:2, :].rearrange("p a b -> p (a b)")
                            .rearrange("p (a m) -> p a m", m=16),
                        in_=src[:, e_ * 16:(e_ + 1) * 16, :]))
                    prog.new_phase()
                    prog.op("scalar", lambda e: e.copy(
                        out=agbf[:, :, :].rearrange("p a m -> p (a m)"),
                        in_=stage[:, 0:2, :].rearrange("p a b -> p (a b)")))
                    prog.new_phase()
                    for k in range(DKT):
                        prog.op("tensor", lambda e, k=k: e.matmul(
                            pb[5][0:16, :], agbf[:, k, :], x1bf[:, k, :],
                            start=(k == 0), stop=(k == DKT - 1)))
                    prog.new_phase()
                    prog.op("scalar", lambda e, e_=e_, dst=dst: e.copy(
                        out=dst[:, e_, :], in_=pb[5][0:16, :]))
                    prog.new_phase()
            prog.op("vector", lambda e: e.memset(s_acc[:, :, :], 0.0))
            prog.new_phase()
            for ft in range(NFF0):
                fs = slice(ft * 128, (ft + 1) * 128)
                prog.dma("sync", lambda e, ft=ft, gs=gs: e.dma_start(
                    out=gb[:, :], in_=gbr[:, ft, gs]))
                prog.dma("sync", lambda e, ft=ft, gs=gs: e.dma_start(
                    out=ub[:, :], in_=ubr[:, ft, gs]))
                prog.dma("sync", lambda e, fs=fs: e.dma_start(out=bst[:, :, :], in_=bgr[:, :, fs]))
                prog.dma("sync", lambda e, ft=ft: e.dma_start(out=adst[:, :, :],
                                                              in_=adr[:, :, ft, :]))
                prog.new_phase()
                prog.op("vector", lambda e: e.tensor_copy(
                    out=bgbf[:, :, :].rearrange("p e f -> p (e f)"),
                    in_=bst[:, :, :].rearrange("p e f -> p (e f)")))
                prog.op("scalar", lambda e: e.copy(
                    out=adbf[:, :, :].rearrange("p e m -> p (e m)"),
                    in_=adst[:, :, :].rearrange("p e m -> p (e m)")))
                prog.new_phase()
                prog.dma("sync", lambda e, fs=fs: e.dma_start(out=bst[:, :, :], in_=bur[:, :, fs]))
                prog.new_phase()
                prog.op("gpsimd", lambda e: e.tensor_copy(
                    out=bubf[:, :, :].rearrange("p e f -> p (e f)"),
                    in_=bst[:, :, :].rearrange("p e f -> p (e f)")))
                prog.new_phase()
                for e_ in range(NE):
                    prog.op("tensor", lambda e, e_=e_: e.matmul(
                        pb[2][:, :], bgbf[:, e_, :], tg_bf[:, e_, :], start=True, stop=True))
                    prog.op("tensor", lambda e, e_=e_: e.matmul(
                        pb[3][:, :], bubf[:, e_, :], tu_bf[:, e_, :], start=True, stop=True))
                    prog.new_phase()
                    prog.op("vector", lambda e: e.scalar_tensor_tensor(
                        hg[:, :], pb[2][:, :], 2.0, gb[:, :], ALU.mult, ALU.add))
                    prog.new_phase()
                    prog.op("vector", lambda e: e.scalar_tensor_tensor(
                        hu[:, :], pb[3][:, :], 2.0, ub[:, :], ALU.mult, ALU.add))
                    prog.op("scalar", lambda e: e.activation(hg[:, :], hg[:, :], AF.Silu))
                    prog.new_phase()
                    prog.op("vector", lambda e: e.tensor_mul(hbf[:, :], hg[:, :], hu[:, :]))
                    prog.new_phase()
                    prog.op("tensor", lambda e, e_=e_: e.matmul(
                        pb[6][0:16, :], adbf[:, e_, :], hbf[:, :], start=True, stop=True))
                    prog.new_phase()
                    prog.op("vector", lambda e, e_=e_: e.tensor_add(
                        s_acc[:, e_, :], s_acc[:, e_, :], pb[6][0:16, :]))
                    prog.new_phase()
            for e_ in range(NE):
                prog.op("tensor", lambda e, e_=e_: e.matmul(
                    pb[7][0:16, :], sel[:, e_ * 16:(e_ + 1) * 16], we8[:, :],
                    start=True, stop=True))
                prog.new_phase()
                prog.op("vector", lambda e, e_=e_: e.tensor_mul(
                    s_bf[:, e_, :], s_acc[:, e_, :], pb[7][0:16, :]))
                prog.new_phase()
            for dt in range(DKT):
                ds = slice(dt * 128, (dt + 1) * 128)
                prog.dma("sync", lambda e, ds=ds: e.dma_start(out=bst[:, :, :], in_=bdr[:, :, ds]))
                prog.dma("sync", lambda e, dt=dt, gs=gs: e.dma_start(
                    out=xres[:, :], in_=ybr[:, dt, gs]))
                prog.new_phase()
                prog.op("scalar", lambda e: e.copy(
                    out=bdbf[:, :, :].rearrange("p e f -> p (e f)"),
                    in_=bst[:, :, :].rearrange("p e f -> p (e f)")))
                prog.new_phase()
                for e_ in range(NE):
                    prog.op("tensor", lambda e, e_=e_: e.matmul(
                        pb[7][:, :], bdbf[:, e_, :], s_bf[:, e_, :],
                        start=(e_ == 0), stop=(e_ == NE - 1)))
                prog.new_phase()
                prog.op("vector", lambda e, dt=dt: e.tensor_add(
                    ybuf[:, dt, :], pb[7][:, :], xres[:, :]))
                prog.new_phase()
            lnc = {"ones": ones, "pb_s1": pb[0], "pb_s2": pb[1], "pb_A": pb[2],
                   "pb_B": pb[3], "stats": stats, "sq": sq, "eps": eps[0:1, 0:1]}

            def out_write(kt, g=g):
                dst = outt[:, kt % 2, :]

                def post(p, kt=kt, g=g):
                    p.dma("gpsimd", lambda e, kt=kt, g=g: e.dma_start(
                        out=x2r[:, kt, g * NT:(g + 1) * NT], in_=outt[:, kt % 2, :]))
                return dst, post
            ln_group(prog, lnc, lambda kt: ybuf[:, kt, :], out_write)
        _finish(nc, st, prog)
    return nc


def build_E():
    nc = bass.Bass("TRN2", target_bir_lowering=False)
    x2e = nc.dram_tensor("x2e", [D, TOK + 2], F32, kind="ExternalInput").ap()
    wuT = nc.dram_tensor("wuT", [D, D], F32, kind="ExternalInput").ap()
    wvT = nc.dram_tensor("wvT", [D, D], F32, kind="ExternalInput").ap()
    wg1T = nc.dram_tensor("wg1T", [D, 512], F32, kind="ExternalInput").ap()
    wg2T = nc.dram_tensor("wg2T", [512, D], F32, kind="ExternalInput").ap()
    bu_pt = nc.dram_tensor("bu_pt", [128, DKT], F32, kind="ExternalInput").ap()
    bv_pt = nc.dram_tensor("bv_pt", [128, DKT], F32, kind="ExternalInput").ap()
    bg1_pt = nc.dram_tensor("bg1_pt", [128, 4], F32, kind="ExternalInput").ap()
    bg2_pt = nc.dram_tensor("bg2_pt", [128, DKT], F32, kind="ExternalInput").ap()
    cw_pt = nc.dram_tensor("cw_pt", [128, DKT, 3], F32, kind="ExternalInput").ap()
    cb_pt = nc.dram_tensor("cb_pt", [128, DKT], F32, kind="ExternalInput").ap()
    toutT = nc.dram_tensor("toutT", [D, TOK], F32, kind="ExternalOutput").ap()
    ugT = nc.dram_tensor("ugT", [D, TOK], F32, kind="ExternalOutput").ap()
    vgT = nc.dram_tensor("vgT", [D, TOK], F32, kind="ExternalOutput").ap()

    x2r = x2e.rearrange("(kt p) t -> p kt t", p=128)
    wur = wuT.rearrange("(kt p) c -> p kt c", p=128)
    wvr = wvT.rearrange("(kt p) c -> p kt c", p=128)
    wg1r = wg1T.rearrange("(kt p) c -> p kt c", p=128)
    wg2r = wg2T.rearrange("(kt p) c -> p kt c", p=128)
    tor = toutT.rearrange("(kt p) t -> p kt t", p=128)
    ugr = ugT.rearrange("(kt p) t -> p kt t", p=128)
    vgr = vgT.rearrange("(kt p) t -> p kt t", p=128)

    prog = Prog()
    with ExitStack() as st:
        sb = lambda n, s, d: st.enter_context(nc.sbuf_tensor(n, s, d))
        stage1 = sb("stage1", [128, TOK + 2], F32)
        x2bf = sb("x2bf", [128, DKT, TOK + 2], BF16)
        tconv = sb("tconv", [128, TOK], F32)
        cw = sb("cw", [128, DKT, 3], F32)
        cb = sb("cb", [128, DKT], F32)
        bu = sb("bu", [128, DKT], F32)
        bv = sb("bv", [128, DKT], F32)
        bg1 = sb("bg1", [128, 4], F32)
        bg2 = sb("bg2", [128, DKT], F32)
        wsl = sb("wsl", [128, 8, 128], F32)
        wbf = sb("wbf", [128, DKT, 128], BF16)
        hbf = sb("hbf", [128, 4, TOK], BF16)
        gT = sb("gT", [128, DKT, TOK], F32)
        oscr = sb("oscr", [128, TOK], F32)
        pb = [st.enter_context(nc.psum_tensor("pb%d" % i, [128, 512], F32)) for i in range(2)]

        prog.dma("sync", lambda e: e.dma_start(out=cw[:, :, :], in_=cw_pt[:, :, :]))
        prog.dma("sync", lambda e: e.dma_start(out=cb[:, :], in_=cb_pt[:, :]))
        prog.dma("sync", lambda e: e.dma_start(out=bu[:, :], in_=bu_pt[:, :]))
        prog.dma("sync", lambda e: e.dma_start(out=bv[:, :], in_=bv_pt[:, :]))
        prog.dma("sync", lambda e: e.dma_start(out=bg1[:, :], in_=bg1_pt[:, :]))
        prog.dma("sync", lambda e: e.dma_start(out=bg2[:, :], in_=bg2_pt[:, :]))
        prog.new_phase()
        for kt in range(DKT):
            prog.dma("sync", lambda e, kt=kt: e.dma_start(out=stage1[:, :], in_=x2r[:, kt, :]))
            prog.new_phase()
            prog.op("scalar", lambda e, kt=kt: e.copy(out=x2bf[:, kt, :], in_=stage1[:, :]))
            prog.op("vector", lambda e, kt=kt: e.tensor_scalar_mul(
                tconv[:, :], stage1[:, 0:TOK], cw[:, kt, 0:1]))
            prog.new_phase()
            prog.op("vector", lambda e, kt=kt: e.scalar_tensor_tensor(
                tconv[:, :], stage1[:, 1:TOK + 1], cw[:, kt, 1:2], tconv[:, :],
                ALU.mult, ALU.add))
            prog.new_phase()
            prog.op("vector", lambda e, kt=kt: e.scalar_tensor_tensor(
                tconv[:, :], stage1[:, 2:TOK + 2], cw[:, kt, 2:3], tconv[:, :],
                ALU.mult, ALU.add))
            prog.new_phase()
            prog.op("vector", lambda e, kt=kt: e.tensor_scalar_add(
                tconv[:, :], tconv[:, :], cb[:, kt:kt + 1]))
            prog.new_phase()
            prog.dma("gpsimd", lambda e, kt=kt: e.dma_start(out=tor[:, kt, :], in_=tconv[:, :]))
            prog.new_phase()
        for ht in range(4):
            for hh in range(2):
                prog.dma("sync", lambda e, ht=ht, hh=hh: e.dma_start(
                    out=wsl[:, :, :],
                    in_=wg1r[:, hh * 8:(hh + 1) * 8, ht * 128:(ht + 1) * 128]))
                prog.new_phase()
                prog.op("scalar", lambda e, hh=hh: e.copy(
                    out=wbf[:, hh * 8:(hh + 1) * 8, :].rearrange("p a b -> p (a b)"),
                    in_=wsl[:, :, :].rearrange("p a b -> p (a b)")))
                prog.new_phase()
            for g in range(2):
                for k in range(DKT):
                    prog.op("tensor", lambda e, g=g, k=k: e.matmul(
                        pb[g][:, :], wbf[:, k, :],
                        x2bf[:, k, 1 + g * 512:1 + (g + 1) * 512],
                        start=(k == 0), stop=(k == DKT - 1)))
            prog.new_phase()
            for g in range(2):
                prog.op("scalar", lambda e, ht=ht, g=g: e.activation(
                    hbf[:, ht, g * 512:(g + 1) * 512], pb[g][:, :], AF.Relu,
                    bias=bg1[:, ht:ht + 1]))
            prog.new_phase()
        for ct in range(DKT):
            prog.dma("sync", lambda e, ct=ct: e.dma_start(
                out=wsl[:, 0:4, :], in_=wg2r[:, :, ct * 128:(ct + 1) * 128]))
            prog.new_phase()
            prog.op("scalar", lambda e: e.copy(
                out=wbf[:, 0:4, :].rearrange("p a b -> p (a b)"),
                in_=wsl[:, 0:4, :].rearrange("p a b -> p (a b)")))
            prog.new_phase()
            for g in range(2):
                for k in range(4):
                    prog.op("tensor", lambda e, g=g, k=k: e.matmul(
                        pb[g][:, :], wbf[:, k, :], hbf[:, k, g * 512:(g + 1) * 512],
                        start=(k == 0), stop=(k == 3)))
            prog.new_phase()
            for g in range(2):
                prog.op("scalar", lambda e, ct=ct, g=g: e.activation(
                    gT[:, ct, g * 512:(g + 1) * 512], pb[g][:, :], AF.Sigmoid,
                    bias=bg2[:, ct:ct + 1]))
            prog.new_phase()
        for (wr_ap, b_sb, out_r) in ((wur, bu, ugr), (wvr, bv, vgr)):
            for ct in range(DKT):
                for hh in range(2):
                    prog.dma("sync", lambda e, ct=ct, hh=hh, wr_ap=wr_ap: e.dma_start(
                        out=wsl[:, :, :],
                        in_=wr_ap[:, hh * 8:(hh + 1) * 8, ct * 128:(ct + 1) * 128]))
                    prog.new_phase()
                    prog.op("scalar", lambda e, hh=hh: e.copy(
                        out=wbf[:, hh * 8:(hh + 1) * 8, :].rearrange("p a b -> p (a b)"),
                        in_=wsl[:, :, :].rearrange("p a b -> p (a b)")))
                    prog.new_phase()
                for g in range(2):
                    for k in range(DKT):
                        prog.op("tensor", lambda e, g=g, k=k: e.matmul(
                            pb[g][:, :], wbf[:, k, :],
                            x2bf[:, k, 1 + g * 512:1 + (g + 1) * 512],
                            start=(k == 0), stop=(k == DKT - 1)))
                prog.new_phase()
                for g in range(2):
                    prog.op("vector", lambda e, ct=ct, g=g, b_sb=b_sb: e.scalar_tensor_tensor(
                        oscr[:, g * 512:(g + 1) * 512], pb[g][:, :], b_sb[:, ct:ct + 1],
                        gT[:, ct, g * 512:(g + 1) * 512], ALU.add, ALU.mult))
                prog.new_phase()
                prog.dma("gpsimd", lambda e, ct=ct, out_r=out_r: e.dma_start(
                    out=out_r[:, ct, :], in_=oscr[:, :]))
                prog.new_phase()
        _finish(nc, st, prog)
    return nc


def build_F():
    nc = bass.Bass("TRN2", target_bir_lowering=False)
    ug = nc.dram_tensor("ug", [256, T], F32, kind="ExternalInput").ap()
    vg = nc.dram_tensor("vg", [256, T], F32, kind="ExternalInput").ap()
    go = nc.dram_tensor("go", [256, T], F32, kind="ExternalOutput").ap()
    ugr = ug.rearrange("(h p) t -> p h t", p=128)
    vgr = vg.rearrange("(h p) t -> p h t", p=128)
    gor = go.rearrange("(h p) t -> p h t", p=128)

    prog = Prog()
    with ExitStack() as st:
        sb = lambda n: st.enter_context(nc.sbuf_tensor(n, [128, T], F32))
        u_sb, v_sb, cs = sb("u_sb"), sb("v_sb"), sb("cs")
        for h in range(2):
            prog.dma("sync", lambda e, h=h: e.dma_start(out=u_sb[:, :], in_=ugr[:, h, :]))
            prog.dma("sync", lambda e, h=h: e.dma_start(out=v_sb[:, :], in_=vgr[:, h, :]))
            prog.new_phase()
            prog.op("vector", lambda e: e.tensor_tensor_scan(
                cs[:, :], u_sb[:, :], u_sb[:, :], 0.0, ALU.add, ALU.bypass))
            prog.new_phase()
            prog.op("vector", lambda e: e.tensor_mul(cs[:, :], cs[:, :], v_sb[:, :]))
            prog.new_phase()
            prog.dma("gpsimd", lambda e, h=h: e.dma_start(out=gor[:, h, :], in_=cs[:, :]))
            prog.new_phase()
        _finish(nc, st, prog)
    return nc


def build_G():
    nc = bass.Bass("TRN2", target_bir_lowering=False)
    x2T = nc.dram_tensor("x2T", [D, TOK], F32, kind="ExternalInput").ap()
    toT = nc.dram_tensor("toT", [D, TOK], F32, kind="ExternalInput").ap()
    goT = nc.dram_tensor("goT", [D, TOK], F32, kind="ExternalInput").ap()
    wuT = nc.dram_tensor("wuT", [D, D], F32, kind="ExternalInput").ap()
    ones_d = nc.dram_tensor("ones", [128, 128], F32, kind="ExternalInput").ap()
    x3T = nc.dram_tensor("x3T", [D, TOK], F32, kind="ExternalOutput").ap()

    x2r = x2T.rearrange("(kt p) t -> p kt t", p=128)
    tor = toT.rearrange("(kt p) t -> p kt t", p=128)
    gor = goT.rearrange("(kt p) t -> p kt t", p=128)
    wur = wuT.rearrange("(kt p) c -> p kt c", p=128)
    x3r = x3T.rearrange("(kt p) t -> p kt t", p=128)

    prog = Prog()
    with ExitStack() as st:
        sb = lambda n, s, d: st.enter_context(nc.sbuf_tensor(n, s, d))
        x2sl = sb("x2sl", [128, DKT, TOK], F32)
        tsl = sb("tsl", [128, DKT, TOK], F32)
        gobf = sb("gobf", [128, DKT, TOK], BF16)
        stage = sb("stage", [128, TOK], F32)
        wsl = sb("wsl", [128, 8, 128], F32)
        wbf = sb("wbf", [128, DKT, 128], BF16)
        scr = sb("scr", [128, 2, 512], F32)
        sq = [sb("sq0", [128, NT], F32), sb("sq1", [128, NT], F32)]
        stats = sb("stats", [1, 4 * NT], F32)
        ones = sb("onesb", [128, 128], F32)
        eps = sb("eps", [1, 1], F32)
        pb = [st.enter_context(nc.psum_tensor("pb%d" % i, [128, 512], F32)) for i in range(6)]

        prog.op("vector", lambda e: e.memset(eps[:, :], EPS))
        for ch in range(4):
            prog.dma("sync", lambda e, ch=ch: e.dma_start(
                out=x2sl[:, ch * 4:(ch + 1) * 4, :], in_=x2r[:, ch * 4:(ch + 1) * 4, :]))
            prog.dma("sync", lambda e, ch=ch: e.dma_start(
                out=tsl[:, ch * 4:(ch + 1) * 4, :], in_=tor[:, ch * 4:(ch + 1) * 4, :]))
        prog.dma("sync", lambda e: e.dma_start(out=ones[:, :], in_=ones_d[:, :]))
        prog.new_phase()
        for kt in range(DKT):
            prog.dma("sync", lambda e, kt=kt: e.dma_start(out=stage[:, :], in_=gor[:, kt, :]))
            prog.new_phase()
            prog.op("scalar", lambda e, kt=kt: e.copy(out=gobf[:, kt, :], in_=stage[:, :]))
            prog.new_phase()
        for ct in range(DKT):
            for hh in range(2):
                prog.dma("sync", lambda e, ct=ct, hh=hh: e.dma_start(
                    out=wsl[:, :, :],
                    in_=wur[:, hh * 8:(hh + 1) * 8, ct * 128:(ct + 1) * 128]))
                prog.new_phase()
                prog.op("scalar", lambda e, hh=hh: e.copy(
                    out=wbf[:, hh * 8:(hh + 1) * 8, :].rearrange("p a b -> p (a b)"),
                    in_=wsl[:, :, :].rearrange("p a b -> p (a b)")))
                prog.new_phase()
            for g in range(2):
                for k in range(DKT):
                    prog.op("tensor", lambda e, g=g, k=k: e.matmul(
                        pb[g][:, :], wbf[:, k, :], gobf[:, k, g * 512:(g + 1) * 512],
                        start=(k == 0), stop=(k == DKT - 1)))
            prog.new_phase()
            for g in range(2):
                prog.op("vector", lambda e, ct=ct, g=g: e.tensor_add(
                    scr[:, g, :], pb[g][:, :], tsl[:, ct, g * 512:(g + 1) * 512]))
            prog.new_phase()
            for g in range(2):
                prog.op("gpsimd", lambda e, ct=ct, g=g: e.tensor_add(
                    x2sl[:, ct, g * 512:(g + 1) * 512],
                    x2sl[:, ct, g * 512:(g + 1) * 512], scr[:, g, :]))
            prog.new_phase()
        lnc = {"ones": ones, "pb_s1": pb[2], "pb_s2": pb[3], "pb_A": pb[4], "pb_B": pb[5],
               "stats": stats, "sq": sq, "eps": eps[0:1, 0:1]}
        for g in range(NG):
            def out_write(kt, g=g):
                return tsl[:, kt, g * NT:(g + 1) * NT], None
            ln_group(prog, lnc, lambda kt, g=g: x2sl[:, kt, g * NT:(g + 1) * NT], out_write)
        for ch in range(4):
            prog.dma("gpsimd", lambda e, ch=ch: e.dma_start(
                out=x3r[:, ch * 4:(ch + 1) * 4, :], in_=tsl[:, ch * 4:(ch + 1) * 4, :]))
        prog.new_phase()
        _finish(nc, st, prog)
    return nc


def build_H():
    nc = bass.Bass("TRN2", target_bir_lowering=False)
    x3T = nc.dram_tensor("x3T", [D, TOK], F32, kind="ExternalInput").ap()
    su_ug = nc.dram_tensor("su_ug", [128, DKT], F32, kind="ExternalInput").ap()
    su_dp = nc.dram_tensor("su_dp", [128, NFF1], F32, kind="ExternalInput").ap()
    wsc_up = nc.dram_tensor("wsc_up", [128, NFF1], F32, kind="ExternalInput").ap()
    wsc_gt = nc.dram_tensor("wsc_gt", [128, NFF1], F32, kind="ExternalInput").ap()
    wsc_dp = nc.dram_tensor("wsc_dp", [128, DKT], F32, kind="ExternalInput").ap()
    wupT = nc.dram_tensor("wupT", [D, FF1], F32, kind="ExternalInput").ap()
    wgtT = nc.dram_tensor("wgtT", [D, FF1], F32, kind="ExternalInput").ap()
    wdnT = nc.dram_tensor("wdnT", [FF1, D], F32, kind="ExternalInput").ap()
    h128_d = nc.dram_tensor("h128", [128, 128], F32, kind="ExternalInput").ap()
    ones_d = nc.dram_tensor("ones", [128, 128], F32, kind="ExternalInput").ap()
    x4T = nc.dram_tensor("x4T", [D, TOK], F32, kind="ExternalOutput").ap()

    x3r = x3T.rearrange("(kt p) t -> p kt t", p=128)
    wupr = wupT.rearrange("(kt p) f -> p kt f", p=128)
    wgtr = wgtT.rearrange("(kt p) f -> p kt f", p=128)
    wdnr = wdnT.rearrange("(ft p) d -> p ft d", p=128)
    x4r = x4T.rearrange("(kt p) t -> p kt t", p=128)

    C1 = 1.0 / (np.sqrt(2048.0) * 64.0)
    C2 = 1.0 / (np.sqrt(8192.0) * 64.0)

    prog = Prog()
    with ExitStack() as st:
        sb = lambda n, s, d: st.enter_context(nc.sbuf_tensor(n, s, d))
        zmid = sb("zmid", [128, NFF1, NT], BF16)
        z1f = sb("z1f", [128, DKT, NT], F32)
        z1bf = sb("z1bf", [128, DKT, NT], BF16)
        tmpb = sb("tmpb", [128, 8, NT], F32)
        stage = sb("stage", [128, DKT, 128], F32)
        wbf_a = sb("wbf_a", [128, DKT, 128], BF16)
        wdnbf = sb("wdnbf", [128, NFF1, 128], BF16)
        xs = sb("xs", [128, NT], F32)
        xsbf = sb("xsbf", [128, NT], BF16)
        upsb = sb("upsb", [128, NT], F32)
        sgsb = sb("sgsb", [128, NT], F32)
        xres = sb("xres", [128, NT], F32)
        su1 = sb("su1", [128, DKT], F32)
        su2 = sb("su2", [128, NFF1], F32)
        w64u = sb("w64u", [128, NFF1], F32)
        w64g = sb("w64g", [128, NFF1], F32)
        w64d = sb("w64d", [128, DKT], F32)
        h128 = sb("h128b", [128, 128], F32)
        h128bf = sb("h128bf", [128, 128], BF16)
        sq = [sb("sq0", [128, NT], F32), sb("sq1", [128, NT], F32)]
        stats = sb("stats", [1, 4 * NT], F32)
        ones = sb("onesb", [128, 128], F32)
        outt = sb("outt", [128, 2, NT], F32)
        eps = sb("eps", [1, 1], F32)
        pb = [st.enter_context(nc.psum_tensor("pb%d" % i, [128, 512], F32)) for i in range(6)]

        prog.op("vector", lambda e: e.memset(eps[:, :], EPS))
        prog.dma("sync", lambda e: e.dma_start(out=su1[:, :], in_=su_ug[:, :]))
        prog.dma("sync", lambda e: e.dma_start(out=su2[:, :], in_=su_dp[:, :]))
        prog.dma("sync", lambda e: e.dma_start(out=w64u[:, :], in_=wsc_up[:, :]))
        prog.dma("sync", lambda e: e.dma_start(out=w64g[:, :], in_=wsc_gt[:, :]))
        prog.dma("sync", lambda e: e.dma_start(out=w64d[:, :], in_=wsc_dp[:, :]))
        prog.dma("sync", lambda e: e.dma_start(out=h128[:, :], in_=h128_d[:, :]))
        prog.dma("sync", lambda e: e.dma_start(out=ones[:, :], in_=ones_d[:, :]))
        prog.new_phase()
        prog.op("scalar", lambda e: e.mul(w64u[:, :], w64u[:, :], 64.0))
        prog.op("vector", lambda e: e.tensor_scalar_mul(w64g[:, :], w64g[:, :], 64.0))
        prog.op("gpsimd", lambda e: e.tensor_scalar_mul(w64d[:, :], w64d[:, :], 64.0))
        prog.new_phase()
        prog.op("scalar", lambda e: e.copy(out=h128bf[:, :], in_=h128[:, :]))
        prog.new_phase()

        for g in range(NG):
            gs = slice(g * NT, (g + 1) * NT)
            for kt in range(DKT):
                prog.dma("sync", lambda e, kt=kt, gs=gs: e.dma_start(
                    out=xs[:, :], in_=x3r[:, kt, gs]))
                prog.new_phase()
                prog.op("vector", lambda e, kt=kt: e.tensor_scalar_mul(
                    xs[:, :], xs[:, :], su1[:, kt:kt + 1]))
                prog.new_phase()
                prog.op("scalar", lambda e: e.copy(out=xsbf[:, :], in_=xs[:, :]))
                prog.new_phase()
                prog.op("tensor", lambda e: e.matmul(
                    pb[0][:, :], h128bf[:, :], xsbf[:, :], start=True, stop=True))
                prog.new_phase()
                prog.op("scalar", lambda e, kt=kt: e.activation(
                    z1f[:, kt, :], pb[0][:, :], AF.Identity, scale=C1))
                prog.new_phase()
            s = 1
            while s < DKT:
                pairs = [(i, i + s) for i in range(DKT) if (i // s) % 2 == 0]
                for ci in range(0, len(pairs), 8):
                    chunk = pairs[ci:ci + 8]
                    for pi, (a, b) in enumerate(chunk):
                        prog.op("vector", lambda e, a=a, b=b, pi=pi: e.tensor_sub(
                            tmpb[:, pi, :], z1f[:, a, :], z1f[:, b, :]))
                    prog.new_phase()
                    for pi, (a, b) in enumerate(chunk):
                        prog.op("vector", lambda e, a=a, b=b: e.tensor_add(
                            z1f[:, a, :], z1f[:, a, :], z1f[:, b, :]))
                    prog.new_phase()
                    for pi, (a, b) in enumerate(chunk):
                        prog.op("gpsimd", lambda e, b=b, pi=pi: e.tensor_copy(
                            out=z1f[:, b, :], in_=tmpb[:, pi, :]))
                    prog.new_phase()
                s *= 2
            prog.op("scalar", lambda e: e.copy(
                out=z1bf[:, :, :].rearrange("p a t -> p (a t)"),
                in_=z1f[:, :, :].rearrange("p a t -> p (a t)")))
            prog.new_phase()
            for ft in range(NFF1):
                fs = slice(ft * 128, (ft + 1) * 128)
                prog.dma("sync", lambda e, fs=fs: e.dma_start(
                    out=stage[:, :, :], in_=wupr[:, :, fs]))
                prog.new_phase()
                prog.op("scalar", lambda e: e.copy(
                    out=wbf_a[:, :, :].rearrange("p a b -> p (a b)"),
                    in_=stage[:, :, :].rearrange("p a b -> p (a b)")))
                prog.new_phase()
                prog.dma("sync", lambda e, fs=fs: e.dma_start(
                    out=stage[:, :, :], in_=wgtr[:, :, fs]))
                for k in range(DKT):
                    prog.op("tensor", lambda e, k=k: e.matmul(
                        pb[0][:, :], wbf_a[:, k, :], z1bf[:, k, :],
                        start=(k == 0), stop=(k == DKT - 1)))
                prog.new_phase()
                prog.op("vector", lambda e, ft=ft: e.tensor_scalar_mul(
                    upsb[:, :], pb[0][:, :], w64u[:, ft:ft + 1]))
                prog.op("scalar", lambda e: e.copy(
                    out=wbf_a[:, :, :].rearrange("p a b -> p (a b)"),
                    in_=stage[:, :, :].rearrange("p a b -> p (a b)")))
                prog.new_phase()
                for k in range(DKT):
                    prog.op("tensor", lambda e, k=k: e.matmul(
                        pb[1][:, :], wbf_a[:, k, :], z1bf[:, k, :],
                        start=(k == 0), stop=(k == DKT - 1)))
                prog.new_phase()
                prog.op("scalar", lambda e, ft=ft: e.activation(
                    sgsb[:, :], pb[1][:, :], AF.Silu, scale=w64g[:, ft:ft + 1]))
                prog.new_phase()
                prog.op("vector", lambda e, ft=ft: e.tensor_mul(
                    zmid[:, ft, :], sgsb[:, :], upsb[:, :]))
                prog.new_phase()
            for ft in range(NFF1):
                prog.op("vector", lambda e, ft=ft: e.tensor_scalar_mul(
                    zmid[:, ft, :], zmid[:, ft, :], su2[:, ft:ft + 1]))
            prog.new_phase()
            for ft in range(NFF1 + 1):
                if ft < NFF1:
                    prog.op("tensor", lambda e, ft=ft: e.matmul(
                        pb[0 + ft % 2][:, :], h128bf[:, :], zmid[:, ft, :],
                        start=True, stop=True))
                if ft > 0:
                    prog.op("scalar", lambda e, ft=ft: e.copy(
                        out=zmid[:, ft - 1, :], in_=pb[0 + (ft - 1) % 2][:, :]))
                prog.new_phase()
            s = 1
            while s < NFF1:
                pairs = [(i, i + s) for i in range(NFF1) if (i // s) % 2 == 0]
                for ci in range(0, len(pairs), 8):
                    chunk = pairs[ci:ci + 8]
                    for pi, (a, b) in enumerate(chunk):
                        prog.op("vector", lambda e, a=a, b=b, pi=pi: e.tensor_sub(
                            tmpb[:, pi, :], zmid[:, a, :], zmid[:, b, :]))
                    prog.new_phase()
                    for pi, (a, b) in enumerate(chunk):
                        prog.op("vector", lambda e, a=a, b=b: e.tensor_add(
                            zmid[:, a, :], zmid[:, a, :], zmid[:, b, :]))
                    prog.new_phase()
                    for pi, (a, b) in enumerate(chunk):
                        prog.op("gpsimd", lambda e, b=b, pi=pi: e.tensor_copy(
                            out=zmid[:, b, :], in_=tmpb[:, pi, :]))
                    prog.new_phase()
                s *= 2
            for dt in range(DKT):
                ds = slice(dt * 128, (dt + 1) * 128)
                for ch in range(4):
                    ftsl = slice(ch * 16, (ch + 1) * 16)
                    prog.dma("sync", lambda e, ds=ds, ftsl=ftsl: e.dma_start(
                        out=stage[:, :, :], in_=wdnr[:, ftsl, ds]))
                    prog.new_phase()
                    prog.op("scalar", lambda e, ftsl=ftsl: e.copy(
                        out=wdnbf[:, ftsl, :].rearrange("p a b -> p (a b)"),
                        in_=stage[:, :, :].rearrange("p a b -> p (a b)")))
                    prog.new_phase()
                prog.dma("sync", lambda e, dt=dt, gs=gs: e.dma_start(
                    out=xres[:, :], in_=x3r[:, dt, gs]))
                prog.new_phase()
                for ft in range(NFF1):
                    prog.op("tensor", lambda e, ft=ft: e.matmul(
                        pb[2][:, :], wdnbf[:, ft, :], zmid[:, ft, :],
                        start=(ft == 0), stop=(ft == NFF1 - 1)))
                prog.new_phase()
                prog.op("vector", lambda e, dt=dt: e.tensor_scalar(
                    sgsb[:, :], pb[2][:, :], w64d[:, dt:dt + 1], C2, ALU.mult, ALU.mult))
                prog.new_phase()
                prog.op("gpsimd", lambda e, dt=dt: e.tensor_add(
                    z1f[:, dt, :], sgsb[:, :], xres[:, :]))
                prog.new_phase()
            lnc = {"ones": ones, "pb_s1": pb[0], "pb_s2": pb[1], "pb_A": pb[3],
                   "pb_B": pb[4], "stats": stats, "sq": sq, "eps": eps[0:1, 0:1]}

            def out_write(kt, g=g):
                dst = outt[:, kt % 2, :]

                def post(p, kt=kt, g=g):
                    p.dma("gpsimd", lambda e, kt=kt, g=g: e.dma_start(
                        out=x4r[:, kt, g * NT:(g + 1) * NT], in_=outt[:, kt % 2, :]))
                return dst, post
            ln_group(prog, lnc, lambda kt: z1f[:, kt, :], out_write)
        _finish(nc, st, prog)
    return nc


def _ct(a):
    return np.ascontiguousarray(a)


def _pt(v):
    return _ct(v.reshape(-1, 128).T)


def _run(nc, in_maps):
    res = run_bass_kernel_spmd(nc, in_maps, core_ids=list(range(NCORES)))
    return res.results


def kernel(**inp):
    f32 = np.float32
    x = np.asarray(inp["L0_attn_x"], f32)[0]
    xT = _ct(x.T)
    ones = np.ones((128, 128), f32)
    ident = np.eye(128, dtype=f32)

    wactT = _ct(np.asarray(inp["L0_attn_W_act"], f32).T)
    wrecT = _ct(np.asarray(inp["L0_attn_W_rec"], f32).T)
    xT_pad = np.concatenate([np.zeros((D, 1), f32), xT], axis=1)
    ncA = build_A()
    maps = [{"x_ext": _ct(xT_pad[:, c * TOK:c * TOK + TOK + 1]),
             "wactT": wactT, "wrecT": wrecT} for c in range(NCORES)]
    rA = _run(ncA, maps)
    xv = np.concatenate([r["xvT"] for r in rA], axis=1)
    r4 = np.concatenate([r["r4T"] for r in rA], axis=1)

    re_full = _ct(xv[0::2])
    im_full = _ct(xv[1::2])
    ncB = build_B()
    maps = [{"re_in": _ct(re_full[c * 128:(c + 1) * 128]),
             "im_in": _ct(im_full[c * 128:(c + 1) * 128])} for c in range(NCORES)]
    rB = _run(ncB, maps)
    kv = np.empty((D, T), f32)
    kv[0::2] = np.concatenate([r["re_out"] for r in rB], axis=0)
    kv[1::2] = np.concatenate([r["im_out"] for r in rB], axis=0)

    wvalT = _ct(np.asarray(inp["L0_attn_W_val"], f32).T)
    ncC = build_C()
    maps = [{"xT": _ct(xT[:, c * TOK:(c + 1) * TOK]),
             "kvT": _ct(kv[:, c * TOK:(c + 1) * TOK]),
             "r4T": _ct(r4[:, c * TOK:(c + 1) * TOK]),
             "wvalT": wvalT, "ones": ones} for c in range(NCORES)]
    rC = _run(ncC, maps)
    x1 = np.concatenate([r["x1T"] for r in rC], axis=1)

    wgT = _ct(np.asarray(inp["L0_ffn_Wg"], f32).T)
    wuT0 = _ct(np.asarray(inp["L0_ffn_Wu"], f32).T)
    wdT = _ct(np.asarray(inp["L0_ffn_Wd"], f32).T)
    ncD1 = build_D1()
    maps = [{"x1T": _ct(x1[:, c * TOK:(c + 1) * TOK]), "wgT": wgT, "wuT": wuT0,
             "wdT": wdT} for c in range(NCORES)]
    rD1 = _run(ncD1, maps)
    gbF = [r["gbT"] for r in rD1]
    ubF = [r["ubT"] for r in rD1]
    ybF = [r["ybT"] for r in rD1]

    wrT = _ct(np.asarray(inp["L0_ffn_Wr"], f32).T)
    agT = _ct(np.asarray(inp["L0_ffn_Ag"], f32).transpose(0, 2, 1).reshape(NE * D, 16))
    auT = _ct(np.asarray(inp["L0_ffn_Au"], f32).transpose(0, 2, 1).reshape(NE * D, 16))
    bgT = _ct(np.asarray(inp["L0_ffn_Bg"], f32).transpose(0, 2, 1).reshape(NE * 16, FF0))
    buT = _ct(np.asarray(inp["L0_ffn_Bu"], f32).transpose(0, 2, 1).reshape(NE * 16, FF0))
    adT = _ct(np.asarray(inp["L0_ffn_Ad"], f32).transpose(0, 2, 1).reshape(NE * FF0, 16))
    bdT = _ct(np.asarray(inp["L0_ffn_Bd"], f32).transpose(0, 2, 1).reshape(NE * 16, D))
    sel = np.zeros((8, 128), f32)
    for e_ in range(NE):
        sel[e_, e_ * 16:(e_ + 1) * 16] = 2.0
    ncD2 = build_D2()
    maps = [{"x1T": _ct(x1[:, c * TOK:(c + 1) * TOK]), "gbT": gbF[c], "ubT": ubF[c],
             "ybT": ybF[c], "wrT": wrT, "agT": agT, "auT": auT, "bgT": bgT,
             "buT": buT, "adT": adT, "bdT": bdT, "sel": sel, "ident": ident,
             "ones": ones} for c in range(NCORES)]
    rD2 = _run(ncD2, maps)
    x2 = np.concatenate([r["x2T"] for r in rD2], axis=1)

    wuT1 = _ct(np.asarray(inp["L1_attn_Wu"], f32).T)
    wvT1 = _ct(np.asarray(inp["L1_attn_Wv"], f32).T)
    wg1T = _ct(np.asarray(inp["L1_attn_Wg1"], f32).T)
    wg2T = _ct(np.asarray(inp["L1_attn_Wg2"], f32).T)
    x2_pad = np.concatenate([np.zeros((D, 1), f32), x2, np.zeros((D, 1), f32)], axis=1)
    ncE = build_E()
    maps = [{"x2e": _ct(x2_pad[:, c * TOK:c * TOK + TOK + 2]),
             "wuT": wuT1, "wvT": wvT1, "wg1T": wg1T, "wg2T": wg2T,
             "bu_pt": _pt(np.asarray(inp["L1_attn_bu"], f32)),
             "bv_pt": _pt(np.asarray(inp["L1_attn_bv"], f32)),
             "bg1_pt": _pt(np.asarray(inp["L1_attn_bg1"], f32)),
             "bg2_pt": _pt(np.asarray(inp["L1_attn_bg2"], f32)),
             "cw_pt": _ct(np.asarray(inp["L1_attn_conv_w"], f32).reshape(16, 128, 3)
                          .transpose(1, 0, 2)),
             "cb_pt": _pt(np.asarray(inp["L1_attn_conv_b"], f32))} for c in range(NCORES)]
    rE = _run(ncE, maps)
    tout = np.concatenate([r["toutT"] for r in rE], axis=1)
    ug = np.concatenate([r["ugT"] for r in rE], axis=1)
    vg = np.concatenate([r["vgT"] for r in rE], axis=1)

    ncF = build_F()
    maps = [{"ug": _ct(ug[c * 256:(c + 1) * 256]),
             "vg": _ct(vg[c * 256:(c + 1) * 256])} for c in range(NCORES)]
    rF = _run(ncF, maps)
    go = np.concatenate([r["go"] for r in rF], axis=0)

    ncG = build_G()
    maps = [{"x2T": _ct(x2[:, c * TOK:(c + 1) * TOK]),
             "toT": _ct(tout[:, c * TOK:(c + 1) * TOK]),
             "goT": _ct(go[:, c * TOK:(c + 1) * TOK]),
             "wuT": wuT1, "ones": ones} for c in range(NCORES)]
    rG = _run(ncG, maps)
    x3 = np.concatenate([r["x3T"] for r in rG], axis=1)

    wupT = _ct(np.asarray(inp["L1_ffn_W_up"], f32).T)
    wgtT = _ct(np.asarray(inp["L1_ffn_W_gate"], f32).T)
    wdnT = _ct(np.asarray(inp["L1_ffn_W_down"], f32).T)
    h128 = np.ones((1, 1), f32)
    while h128.shape[0] < 128:
        h128 = np.block([[h128, h128], [h128, -h128]])
    h128 = _ct(h128.astype(f32))
    wsug = np.asarray(inp["L1_ffn_Wscale_ug"], f32)
    ncH = build_H()
    maps = [{"x3T": _ct(x3[:, c * TOK:(c + 1) * TOK]),
             "su_ug": _pt(np.asarray(inp["L1_ffn_SU_ug"], f32)),
             "su_dp": _pt(np.asarray(inp["L1_ffn_SU_dp"], f32)),
             "wsc_up": _pt(wsug[:FF1]), "wsc_gt": _pt(wsug[FF1:]),
             "wsc_dp": _pt(np.asarray(inp["L1_ffn_Wscale_dp"], f32)),
             "wupT": wupT, "wgtT": wgtT, "wdnT": wdnT,
             "h128": h128, "ones": ones} for c in range(NCORES)]
    rH = _run(ncH, maps)
    x4 = np.concatenate([r["x4T"] for r in rH], axis=1)

    return _ct(x4.T).reshape(1, T, D)
